# version 03
# baseline (speedup 1.0000x reference)
import contextlib
import numpy as np
import concourse.bass as bass
import concourse.mybir as mybir
from concourse.alu_op_type import AluOpType as ALU

F32 = mybir.dt.float32
BF16 = mybir.dt.bfloat16
AF = mybir.ActivationFunctionType
AX = mybir.AxisListType

ENGINES = ("tensor", "vector", "scalar", "gpsimd", "sync")
D = 2048
S = 2048
NDT = 16


class Step:
    __slots__ = ("idx", "engine", "emit", "deps", "sem", "inc", "val_after", "tag")

    def __init__(self, idx, engine, emit, deps, sem, inc, tag=""):
        self.idx, self.engine, self.emit = idx, engine, emit
        self.deps, self.sem, self.inc, self.tag = deps, sem, inc, tag
        self.val_after = 0


class Prog:
    def __init__(self, nc):
        self.nc = nc
        self.steps = []
        self.sem_totals = {}
        self.sem_handles = {}

    SERIAL = False
    SERIAL_TAGS = ()

    def step(self, engine, emit, deps=(), sem=None, inc=1, tag=""):
        if sem is None:
            sem = engine
        deps = [d for d in deps if d is not None]
        if self.steps and (Prog.SERIAL or any(tag.startswith(t) for t in Prog.SERIAL_TAGS)):
            deps = deps + [self.steps[-1]]
        s = Step(len(self.steps), engine, emit, deps, sem, inc, tag)
        self.steps.append(s)
        self.sem_totals[sem] = self.sem_totals.get(sem, 0) + inc
        s.val_after = self.sem_totals[sem]
        return s

    def dma(self, queue, emit, deps=(), n=1, tag="", sem=None):
        if sem is None:
            sem = "dma_" + queue
        else:
            sem = "dma_" + sem
        return self.step(queue, emit, deps, sem=sem, inc=16 * n, tag=tag)

    def last_steps(self):
        last = {}
        for s in self.steps:
            last[s.sem] = s
        return list(last.values())

    def emit_block(self):
        nc = self.nc
        sem_names = sorted(self.sem_totals.keys())
        with contextlib.ExitStack() as es:
            for sn in sem_names:
                self.sem_handles[sn] = es.enter_context(nc.semaphore(sn))
            blk = es.enter_context(nc.Block())
            per_engine = {e: [] for e in ENGINES}
            for s in self.steps:
                per_engine[s.engine].append(s)

            def make_fn(engine, steps, is_sync):
                def fn(eng):
                    waited = {}
                    for s in steps:
                        for d in s.deps:
                            if d.engine == engine and not d.sem.startswith("dma_"):
                                continue
                            tgt = d.val_after
                            if waited.get(d.sem, -1) >= tgt:
                                continue
                            eng.wait_ge(self.sem_handles[d.sem], tgt)
                            waited[d.sem] = tgt
                        ret = s.emit(eng)
                        semh = self.sem_handles[s.sem]
                        if ret is None:
                            raise RuntimeError(f"step {s.tag}: emit returned None")
                        if isinstance(ret, list):
                            tot = 0
                            for instr, inc in ret:
                                instr.then_inc(semh, inc)
                                tot += inc
                            assert tot == s.inc, f"{s.tag}: inc {tot} != {s.inc}"
                        else:
                            ret.then_inc(semh, s.inc)
                    if is_sync:
                        for sn in sem_names:
                            eng.wait_ge(self.sem_handles[sn], self.sem_totals[sn])
                return fn

            for engine in ENGINES:
                steps = per_engine[engine]
                if engine == "sync":
                    blk.sync(make_fn("sync", steps, True))
                elif steps:
                    getattr(blk, engine)(make_fn(engine, steps, False))
        return self


class CB:

    def __init__(self, nc, prog, es):
        self.nc, self.p, self.es = nc, prog, es

    def sb(self, name, shape, dtype):
        return self.es.enter_context(self.nc.sbuf_tensor(name, shape, dtype))

    def ps(self, name, shape, dtype=F32):
        return self.es.enter_context(self.nc.psum_tensor(name, shape, dtype))

    def dram(self, name, shape, dtype=F32):
        return self.nc.dram_tensor(name, shape, dtype, kind="Internal").ap()



def build_ln_phase(cb, x_dram, xn_dram, deps=(), uid="0"):
    nc, p = cb.nc, cb.p
    with contextlib.ExitStack() as es:
        lcb = CB(nc, p, es)
        xb = lcb.sb("ln_xb", [128, NDT * S], BF16)
        sq = lcb.sb("ln_sq", [128, S], BF16)
        ones_col = lcb.sb("ln_ones", [128, 1], BF16)
        eps_col = lcb.sb("ln_eps", [128, 1], F32)
        srow = lcb.sb("ln_srow", [128, 2 * S], F32)
        murep = lcb.sb("ln_murep", [128, S], F32)
        rrep = lcb.sb("ln_rrep", [128, S], F32)
        tmp = lcb.sb("ln_tmp", [128, S], F32)
        xn_t = lcb.sb("ln_xnt", [128, 2 * S], BF16)
        ps_sum = lcb.ps("ln_ps_sum", [128, 2048])
        ps_sum2 = lcb.ps("ln_ps_sum2", [128, 2048])
        rows_d = lcb.dram("ln_rows" + uid, [2, S])

        def em_consts(g):
            g.memset(ones_col[:, :], 1.0)
            return g.memset(eps_col[:, :], 1e-5)
        st_ones = p.step("gpsimd", em_consts, deps, tag="ln_ones")

        loads = []
        for dt in range(NDT):
            def em(g, dt=dt):
                return [(g.dma_start(out=xb[:, dt * S:(dt + 1) * S],
                                     in_=x_dram[dt * 128:(dt + 1) * 128, :]), 16)]
            loads.append(p.dma("gpsimd", em, deps, tag=f"ln_load{dt}", sem="ln_ld"))

        mm_steps = []
        prev_mm = None
        for dt in range(NDT):
            st_sq = p.step("scalar",
                           lambda s, dt=dt: s.activation(out=sq[:, :],
                                                         in_=xb[:, dt * S:(dt + 1) * S],
                                                         func=AF.Square),
                           [loads[-1], prev_mm], tag=f"ln_sq{dt}")

            def em_mm(t, dt=dt):
                last = None
                for c in range(4):
                    sl = slice(c * 512, (c + 1) * 512)
                    last = t.matmul(ps_sum[0:1, sl], ones_col[:, :],
                                    xb[:, dt * S + c * 512: dt * S + (c + 1) * 512],
                                    start=(dt == 0), stop=(dt == NDT - 1))
                    last = t.matmul(ps_sum2[0:1, sl], ones_col[:, :], sq[:, sl],
                                    start=(dt == 0), stop=(dt == NDT - 1))
                return last
            st_mm = p.step("tensor", em_mm, [st_ones, loads[-1], st_sq], tag=f"ln_mm{dt}")
            mm_steps.append(st_mm)
            prev_mm = st_mm

        murow = srow[0:1, 0:S]
        m2row = tmp[0:1, 0:S]
        varrow = srow[0:1, S:2 * S]

        def em_evac(v):
            v.tensor_copy(out=murow, in_=ps_sum[0:1, :])
            return v.tensor_copy(out=varrow, in_=ps_sum2[0:1, :])
        st_evac = p.step("vector", em_evac, [mm_steps[-1]], tag="ln_evac")

        def em_mu(v):
            v.tensor_scalar(out=murow, in0=murow, scalar1=1.0 / D, scalar2=None, op0=ALU.mult)
            v.tensor_tensor(out=m2row, in0=murow, in1=murow, op=ALU.mult)
            return v.scalar_tensor_tensor(out=varrow, in0=varrow, scalar=1.0 / D, in1=m2row,
                                          op0=ALU.mult, op1=ALU.subtract)
        st_mu = p.step("vector", em_mu, [st_evac], tag="ln_mu")

        def em_rstd(s):
            s.activation(out=varrow, in_=varrow, func=AF.Ln, bias=eps_col[0:1, :])
            return s.activation(out=varrow, in_=varrow, func=AF.Exp, scale=-0.5)
        st_rstd = p.step("scalar", em_rstd, [st_mu, st_ones], tag="ln_rstd")

        def em_out(g):
            a = g.dma_start(out=rows_d[0:1, :], in_=murow)
            b = g.dma_start(out=rows_d[1:2, :], in_=varrow)
            return [(a, 16), (b, 16)]
        st_out = p.dma("gpsimd", em_out, [st_rstd, st_mu], n=2, tag="ln_rowout", sem="ln_row")

        def em_rep(g):
            a = g.dma_start(out=murep[:, :], in_=rows_d[0:1, :].broadcast_to([128, S]))
            b = g.dma_start(out=rrep[:, :], in_=rows_d[1:2, :].broadcast_to([128, S]))
            return [(a, 16), (b, 16)]
        st_rep = p.dma("gpsimd", em_rep, [st_out], n=2, tag="ln_rep", sem="ln_row")

        stores = [None, None]
        last_store = None
        for dt in range(NDT):
            buf = dt % 2
            xo = xn_t[:, buf * S:(buf + 1) * S]

            def em_apply(v, dt=dt, xo=xo):
                v.tensor_tensor(out=tmp[:, :], in0=xb[:, dt * S:(dt + 1) * S],
                                in1=murep[:, :], op=ALU.subtract)
                return v.tensor_tensor(out=xo, in0=tmp[:, :], in1=rrep[:, :], op=ALU.mult)
            st_ap = p.step("vector", em_apply, [st_rep, stores[buf]], tag=f"ln_ap{dt}")

            def em_st(sy, dt=dt, xo=xo):
                return [(sy.dma_start(out=xn_dram[dt * 128:(dt + 1) * 128, :], in_=xo), 16)]
            last_store = p.dma("sync", em_st, [st_ap], tag=f"ln_st{dt}", sem="ln_s" + str(dt % 2))
            stores[buf] = last_store
        return p.last_steps()



def build_ffn_phase(cb, xn_dram, out_dram, kind, wg_pm, wu_pm, w2_pm,
                    b1_dram=None, b2_dram=None, deps=(), uid="f0", T=512):
    nc, p = cb.nc, cb.p
    NHT = 64
    NWIN = S // T
    with contextlib.ExitStack() as es:
        l = CB(nc, p, es)
        xnw = l.sb(uid + "_xnw", [128, NDT * T], BF16)
        gbuf = l.sb(uid + "_g", [128, NHT * T], BF16)
        wgb = l.sb(uid + "_wgb", [128, 2 * 2048], BF16)
        wub = l.sb(uid + "_wub", [128, 2 * 2048], BF16) if kind == "swiglu" else None
        wdb = l.sb(uid + "_wdb", [128, 2 * 8192], BF16)
        sg = l.sb(uid + "_sg", [128, 2 * T], BF16)
        osb = l.sb(uid + "_osb", [128, 2 * T], F32)
        b1c = l.sb(uid + "_b1c", [128, NHT], F32) if b1_dram is not None else None
        b2c = l.sb(uid + "_b2c", [128, NDT], F32) if b2_dram is not None else None
        ps_gate = l.ps(uid + "_psg", [128, 2 * T])
        ps_up = l.ps(uid + "_psu", [128, 2 * T]) if kind == "swiglu" else None
        ps_dn = l.ps(uid + "_psd", [128, 2 * T])

        st_b = None
        if b1_dram is not None:
            def em_b(g):
                r = [(g.dma_start(out=b1c[:, :], in_=b1_dram), 16)]
                if b2_dram is not None:
                    r.append((g.dma_start(out=b2c[:, :], in_=b2_dram), 16))
                return r
            st_b = p.dma("gpsimd", em_b, deps, n=2 if b2_dram is not None else 1,
                         tag=uid + "_b", sem=uid + "_b")

        xn_r = xn_dram.rearrange("(dt p) s -> p dt s", p=128)
        prev_win_gate_mm = None
        prev_win_dn_mm = None
        ev_dn = [None, None]
        mm_dn_hist = [None, None]
        silu_hist = [None, None]
        mult_hist = [None, None]
        gate_mm_hist = [None, None]
        up_mm_hist = [None, None]
        wload_hist = [None, None]
        wdload_hist = [None, None]
        for w in range(NWIN):
            tw = slice(w * T, (w + 1) * T)

            def em_xn(g, tw=tw):
                return [(g.dma_start(out=xnw[:, :], in_=xn_r[:, :, tw]), 16)]
            st_xn = p.dma("gpsimd", em_xn, list(deps) + [prev_win_gate_mm],
                          tag=f"{uid}_xn{w}", sem=uid + "_xn")

            last_gate_mm = None
            for ht in range(NHT):
                buf = ht % 2
                wsl = slice(buf * 2048, (buf + 1) * 2048)

                def em_wl(g, ht=ht, wsl=wsl):
                    r = [(g.dma_start(out=wgb[:, wsl], in_=wg_pm[:, ht]), 16)]
                    if kind == "swiglu":
                        r.append((g.dma_start(out=wub[:, wsl], in_=wu_pm[:, ht]), 16))
                    return r
                st_wl = p.dma("gpsimd", em_wl,
                              [gate_mm_hist[buf], up_mm_hist[buf]],
                              n=2 if kind == "swiglu" else 1, tag=f"{uid}_wl{w}_{ht}",
                              sem=uid + "_wl" + str(buf))
                wload_hist[buf] = st_wl

                gsl = slice(buf * T, (buf + 1) * T)

                def em_gmm(t, wsl=wsl, gsl=gsl):
                    last = None
                    for dt in range(NDT):
                        last = t.matmul(ps_gate[:, gsl], wgb[:, wsl][:, dt * 128:(dt + 1) * 128],
                                        xnw[:, dt * T:(dt + 1) * T],
                                        start=(dt == 0), stop=(dt == NDT - 1))
                    return last
                st_gmm = p.step("tensor", em_gmm, [st_wl, st_xn, silu_hist[buf]],
                                tag=f"{uid}_gmm{w}_{ht}")
                gate_mm_hist[buf] = st_gmm
                last_gate_mm = st_gmm

                if kind == "swiglu":
                    def em_umm(t, wsl=wsl, gsl=gsl):
                        last = None
                        for dt in range(NDT):
                            last = t.matmul(ps_up[:, gsl], wub[:, wsl][:, dt * 128:(dt + 1) * 128],
                                            xnw[:, dt * T:(dt + 1) * T],
                                            start=(dt == 0), stop=(dt == NDT - 1))
                        return last
                    st_umm = p.step("tensor", em_umm, [st_wl, st_xn, mult_hist[buf]],
                                    tag=f"{uid}_umm{w}_{ht}")
                    up_mm_hist[buf] = st_umm
                    last_gate_mm = st_umm

                    def em_silu(s, gsl=gsl):
                        return s.activation(out=sg[:, gsl], in_=ps_gate[:, gsl], func=AF.Silu)
                    st_silu = p.step("scalar", em_silu, [st_gmm], tag=f"{uid}_silu{w}_{ht}")
                    silu_hist[buf] = st_silu

                    def em_mult(v, ht=ht, gsl=gsl):
                        return v.tensor_tensor(out=gbuf[:, ht * T:(ht + 1) * T],
                                               in0=sg[:, gsl], in1=ps_up[:, gsl], op=ALU.mult)
                    st_mult = p.step("vector", em_mult,
                                     [st_silu, st_umm, prev_win_dn_mm],
                                     tag=f"{uid}_mul{w}_{ht}")
                    mult_hist[buf] = st_mult
                else:
                    def em_gelu(s, ht=ht, gsl=gsl):
                        return s.activation(out=gbuf[:, ht * T:(ht + 1) * T],
                                            in_=ps_gate[:, gsl], func=AF.Gelu,
                                            bias=b1c[:, ht:ht + 1])
                    st_silu = p.step("scalar", em_gelu,
                                     [st_gmm, st_b, prev_win_dn_mm], tag=f"{uid}_gel{w}_{ht}")
                    silu_hist[buf] = st_silu
                    mult_hist[buf] = st_silu
            prev_win_gate_mm = last_gate_mm

            last_dn_mm = None
            for et in range(NDT):
                buf = et % 2
                wsl = slice(buf * 8192, (buf + 1) * 8192)
                dsl = slice(buf * T, (buf + 1) * T)

                def em_wdl(g, et=et, wsl=wsl):
                    return [(g.dma_start(out=wdb[:, wsl], in_=w2_pm[:, et]), 16)]
                st_wdl = p.dma("gpsimd", em_wdl, [mm_dn_hist[buf]], tag=f"{uid}_wdl{w}_{et}",
                               sem=uid + "_wd" + str(buf))
                wdload_hist[buf] = st_wdl

                def em_dmm(t, wsl=wsl, dsl=dsl):
                    last = None
                    for ht in range(NHT):
                        last = t.matmul(ps_dn[:, dsl], wdb[:, wsl][:, ht * 128:(ht + 1) * 128],
                                        gbuf[:, ht * T:(ht + 1) * T],
                                        start=(ht == 0), stop=(ht == NHT - 1))
                    return last
                st_dmm = p.step("tensor", em_dmm,
                                [st_wdl, mult_hist[0], mult_hist[1], ev_dn[buf]],
                                tag=f"{uid}_dmm{w}_{et}")
                mm_dn_hist[buf] = st_dmm
                last_dn_mm = st_dmm

                def em_ev(v, et=et, dsl=dsl):
                    if b2c is not None:
                        return v.tensor_scalar(out=osb[:, dsl], in0=ps_dn[:, dsl],
                                               scalar1=b2c[:, et:et + 1], scalar2=None,
                                               op0=ALU.add)
                    return v.tensor_copy(out=osb[:, dsl], in_=ps_dn[:, dsl])
                st_ev = p.step("vector", em_ev, [st_dmm, st_b, ev_dn[buf]],
                               tag=f"{uid}_ev{w}_{et}")

                def em_st(sy, et=et, dsl=dsl, tw=tw):
                    return [(sy.dma_start(out=out_dram[et * 128:(et + 1) * 128, tw],
                                          in_=osb[:, dsl]), 16)]
                st_store = p.dma("sync", em_st, [st_ev], tag=f"{uid}_st{w}_{et}",
                                 sem=uid + "_st" + str(buf))
                ev_dn[buf] = st_store
            prev_win_dn_mm = last_dn_mm
        return p.last_steps()


def apx(tile, offset, dims, npart=128, p0=0):
    full = tile.ap() if hasattr(tile, "ap") and not isinstance(tile, bass.AP) else tile
    pstride = full.ap[0][0]
    return bass.AP(tensor=full.tensor, offset=full.offset + p0 * pstride + offset,
                   ap=[[pstride, npart]] + [list(d) for d in dims])





def ecopy(eng, out, in_):
    if hasattr(eng, "tensor_copy"):
        return eng.tensor_copy(out=out, in_=in_)
    return eng.copy(out=out, in_=in_)

def build_l0attn_phase(cb, xn_dram, ca_dram, w, deps=(), debug=None, nch=8, stages='ABC2C3C4DEF', dbgch=0, dbgst2=0):
    nc, p = cb.nc, cb.p
    NCH = 8
    SCS = 256
    SQ = 8.0 ** -0.5
    with contextlib.ExitStack() as es:
        l = CB(nc, p, es)
        qkvw_st = l.sb("a0_qkvst", [16, 48], BF16)
        wblk = l.sb("a0_wblk", [128, 384], BF16)
        projw_st = l.sb("a0_pjst", [16, 16], BF16)
        projbd = l.sb("a0_pjbd", [128, 128], BF16)
        id_bf = l.sb("a0_idbf", [128, 128], BF16)
        id_f = l.sb("a0_idf", [128, 128], F32)
        lep1 = l.sb("a0_lep1", [128, 32], F32)
        lep2 = l.sb("a0_lep2", [128, 32], F32)
        lnw_r = l.sb("a0_lnw", [128, 16], F32)
        lnb_r = l.sb("a0_lnb", [128, 16], F32)
        pb_r = l.sb("a0_pb", [128, 16], F32)
        eps_c = l.sb("a0_eps", [128, 1], F32)
        xnc = l.sb("a0_xnc", [128, NDT * SCS], BF16)
        qkv_s = l.sb("a0_qkvs", [128, 2 * 6144], BF16)
        qkv2_s = l.sb("a0_qkv2s", [128, 2 * 3072], BF16)
        q2wm = l.sb("a0_q2wm", [128, 2048], BF16)
        k2wm = l.sb("a0_k2wm", [128, 2048], BF16)
        v2wm = l.sb("a0_v2wm", [128, 2048], BF16)
        prodb = l.sb("a0_prod", [128, 8192], BF16)
        S1 = l.sb("a0_S1", [128, 2048], F32)
        lepf = l.sb("a0_lepf", [128, 2048], F32)
        Pb = l.sb("a0_P", [128, 2048], BF16)
        mx = l.sb("a0_mx", [128, 256], F32)
        den = l.sb("a0_den", [128, 256], F32)
        catb = l.sb("a0_cat", [128, 2 * 2048], BF16)
        o2cp = l.sb("a0_o2cp", [128, 2048], F32)
        catT = l.sb("a0_catT", [128, 4096], BF16)
        attT = l.sb("a0_attT", [128, 1024], BF16)
        attb = l.sb("a0_att", [128, 2 * 2048], BF16)
        sqb = l.sb("a0_sqb", [128, 2048], BF16)
        ms = l.sb("a0_ms", [128, 128], F32)
        m2 = l.sb("a0_m2", [128, 128], F32)
        fscr = l.sb("a0_fscr", [128, 2048], F32)
        tokf = l.sb("a0_tokf", [128, 2 * 2048], F32)
        ps_a = l.ps("a0_psa", [128, 1024])
        ps_b = l.ps("a0_psb", [128, 1024])
        ps_t = l.ps("a0_pst", [128, 1024])
        ps_u = l.ps("a0_psu", [128, 1024])
        ps_t_bf = ps_t.ap().bitcast(BF16)
        ps_u_bf = ps_u.ap().bitcast(BF16)

        def em_c1(g):
            r = [(g.dma_start(out=qkvw_st[:, :], in_=w["qkv_w"]), 16),
                 (g.dma_start(out=projw_st[:, :], in_=w["proj_w"]), 16)]
            for t in range(3):
                r.append((g.dma_start(out=lep1[:, t * 8:(t + 1) * 8],
                    in_=w["cw0"][:, t:t + 1].rearrange("c one -> one c").broadcast_to([128, 8]),
                    allow_slow_non_contiguous=True), 16))
                r.append((g.dma_start(out=lep2[:, t * 8:(t + 1) * 8],
                    in_=w["cw1"][:, t:t + 1].rearrange("c one -> one c").broadcast_to([128, 8]),
                    allow_slow_non_contiguous=True), 16))
            r.append((g.dma_start(out=lep1[:, 24:32],
                                  in_=w["cb0"].unsqueeze(0).broadcast_to([128, 8])), 16))
            r.append((g.dma_start(out=lep2[:, 24:32],
                                  in_=w["cb1"].unsqueeze(0).broadcast_to([128, 8])), 16))
            r.append((g.dma_start(out=lnw_r[:, :],
                                  in_=w["ln_w"].unsqueeze(0).broadcast_to([128, 16])), 16))
            r.append((g.dma_start(out=lnb_r[:, :],
                                  in_=w["ln_b"].unsqueeze(0).broadcast_to([128, 16])), 16))
            r.append((g.dma_start(out=pb_r[:, :],
                                  in_=w["proj_b"].unsqueeze(0).broadcast_to([128, 16])), 16))
            return r
        st_c1 = p.dma("gpsimd", em_c1, deps, n=13, tag="a0_c1", sem="a0_c")

        def em_c2(g):
            g.memset(wblk[:, :], 0.0)
            g.memset(projbd[:, :], 0.0)
            g.memset(eps_c[:, :], 1e-5)
            g.memset(id_bf[:, :], 1.0)
            g.affine_select(out=id_bf[:, :], in_=id_bf[:, :], pattern=[[-1, 128]],
                            compare_op=ALU.is_equal, fill=0.0, base=0, channel_multiplier=1)
            g.memset(id_f[:, :], 1.0)
            return g.affine_select(out=id_f[:, :], in_=id_f[:, :], pattern=[[-1, 128]],
                                   compare_op=ALU.is_equal, fill=0.0, base=0,
                                   channel_multiplier=1)
        st_c2 = p.step("gpsimd", em_c2, deps, tag="a0_c2")

        def em_c3(g):
            r = []
            for gi in range(8):
                r.append((g.dma_start(out=wblk[16 * gi:16 * gi + 16, 48 * gi:48 * gi + 48],
                                      in_=qkvw_st[:, :]), 16))
                r.append((g.dma_start(out=projbd[16 * gi:16 * gi + 16, 16 * gi:16 * gi + 16],
                                      in_=projw_st[:, :]), 16))
            return r
        st_c3 = p.dma("gpsimd", em_c3, [st_c1, st_c2], n=16, tag="a0_c3", sem="a0_c")

        xn_r = xn_dram.rearrange("(dt p) s -> p dt s", p=128)

        pv = dict(qkv_mm=[None, None], qkv_ev=[None, None], xl=None,
                  b5=[None, None], b6=[None, None], b7=[None, None],
                  c2tr_last=None, c3_last=None, c3red_last=None,
                  c4ev=[None], d1tr=None, d2mm=None, d3tr=None, d3ev=None,
                  e_last=None, a1=[None, None], a5=[None, None], fred=None,
                  scat=[None, None], wm_read=None, catT_read=None, attT_read=None)
        tslot = [None, None]

        dbg = dict(debug or {})

        for ch in range(nch):
            ssl = slice(ch * SCS, (ch + 1) * SCS)

            def em_xl(g, ssl=ssl):
                return [(g.dma_start(out=xnc[:, :], in_=xn_r[:, :, ssl]), 16)]
            st_xl = p.dma("gpsimd", em_xl,
                          list(deps) + [st_c3, pv["qkv_mm"][0], pv["qkv_mm"][1], pv["e_last"]],
                          tag=f"a0_xl{ch}", sem="a0_xl")

            for st2 in range(2):
                for grp in range(8):
                    pi = grp % 2
                    pbuf = (ps_a, ps_b)[pi]

                    def em_mm(t, st2=st2, grp=grp, pbuf=pbuf):
                        last = None
                        for ki in range(2):
                            dt = 2 * grp + ki
                            last = t.matmul(
                                apx(pbuf, ki * 512, [(1, 384)]),
                                xnc[:, dt * SCS + st2 * 128: dt * SCS + st2 * 128 + 128],
                                wblk[:, :], start=True, stop=True)
                        return last
                    st_mm = p.step("tensor", em_mm, [st_xl, st_c3, pv["qkv_ev"][pi]],
                                   tag=f"a0_qmm{ch}_{st2}_{grp}")
                    pv["qkv_mm"][pi] = st_mm

                    def em_ev(s, st2=st2, grp=grp, pbuf=pbuf):
                        s.activation(
                            out=apx(qkv_s, st2 * 6144 + grp * 768, [(1, 768)]),
                            in_=apx(pbuf, 0, [(512, 2), (1, 384)]), func=AF.Copy)
                        last = None
                        for ty in range(3):
                            last = s.activation(
                                out=apx(qkv2_s, st2 * 3072 + ty * 1024 + grp * 16,
                                        [(128, 8), (8, 2), (1, 8)]),
                                in_=apx(pbuf, 8 + 16 * ty, [(1, 8), (512, 2), (48, 8)]),
                                func=AF.Copy)
                        return last
                    adeps = [st_mm]
                    if ch > 0 and st2 == 0 and grp == 0:
                        adeps += [pv["b5"][0], pv["b5"][1], pv["b6"][0], pv["b6"][1],
                                  pv["c2tr_last"], pv["c3_last"]]
                    st_ev = p.step("scalar", em_ev, adeps, tag=f"a0_qev{ch}_{st2}_{grp}")
                    pv["qkv_ev"][pi] = st_ev
            st_qkv_done = pv["qkv_ev"][1]

            if "qkv" in dbg and ch == dbgch:
                p.dma("sync", lambda sy: [(sy.dma_start(out=dbg["qkv"], in_=qkv_s[:, 0:6144]), 16)],
                      [st_qkv_done], tag="a0_dbgq")

            if 'B' not in stages:
                continue
            for st2 in range(2):
                q0 = st2 * 6144

                def em_b1(v, q0=q0):
                    for j in range(8):
                        v.tensor_tensor(
                            out=apx(prodb, 8 * j, [(512, 16), (64, 8), (1, 8)]),
                            in0=apx(qkv_s, q0 + 0, [(384, 16), (48, 8), (1, 8)]),
                            in1=apx(qkv_s, q0 + 16 + 48 * j, [(384, 16), (0, 8), (1, 8)]),
                            op=ALU.mult)
                    return v.tensor_reduce(
                        out=apx(S1, 0, [(1, 1024)]),
                        in_=apx(prodb, 0, [(8, 1024), (1, 8)]),
                        axis=AX.X, op=ALU.add)
                bdeps = [st_qkv_done]
                if st2 == 0:
                    bdeps += [pv["c3_last"], pv["c3red_last"]]
                st_b1 = p.step("vector", em_b1, bdeps, tag=f"a0_b1p{ch}_{st2}")

                def em_b2(v):
                    v.reduce_max(out=apx(mx, 0, [(8, 16), (1, 8)]),
                                 in_=apx(S1, 0, [(64, 16), (8, 8), (1, 8)]), axis=AX.X)
                    return v.tensor_tensor(
                        out=apx(S1, 0, [(64, 16), (8, 8), (1, 8)]),
                        in0=apx(S1, 0, [(64, 16), (8, 8), (1, 8)]),
                        in1=apx(mx, 0, [(8, 16), (1, 8), (0, 8)]), op=ALU.subtract)
                st_b2 = p.step("vector", em_b2, [st_b1], tag=f"a0_b1m{ch}_{st2}")

                st_b3 = p.step("scalar",
                               lambda s: s.activation(out=Pb[:, 0:1024], in_=S1[:, 0:1024],
                                                      func=AF.Exp, scale=SQ),
                               [st_b2], tag=f"a0_b1e{ch}_{st2}")

                def em_b4(v):
                    v.tensor_reduce(out=apx(den, 0, [(8, 16), (1, 8)]),
                                    in_=apx(Pb, 0, [(64, 16), (8, 8), (1, 8)]),
                                    axis=AX.X, op=ALU.add)
                    v.reciprocal(out=den[:, 0:128], in_=den[:, 0:128])
                    return v.tensor_tensor(
                        out=apx(Pb, 0, [(64, 16), (8, 8), (1, 8)]),
                        in0=apx(Pb, 0, [(64, 16), (8, 8), (1, 8)]),
                        in1=apx(den, 0, [(8, 16), (1, 8), (0, 8)]), op=ALU.mult)
                st_b4 = p.step("vector", em_b4, [st_b3], tag=f"a0_b1n{ch}_{st2}")

                def em_b5(v, q0=q0):
                    for c in range(8):
                        v.tensor_tensor(
                            out=apx(prodb, 8 * c, [(512, 16), (64, 8), (1, 8)]),
                            in0=apx(Pb, 0, [(64, 16), (8, 8), (1, 8)]),
                            in1=apx(qkv_s, q0 + 32 + c, [(384, 16), (0, 8), (48, 8)]),
                            op=ALU.mult)
                    return v.tensor_reduce(
                        out=apx(S1, 0, [(1, 1024)]),
                        in_=apx(prodb, 0, [(8, 1024), (1, 8)]),
                        axis=AX.X, op=ALU.add)
                st_b5 = p.step("vector", em_b5, [st_b4], tag=f"a0_b1o{ch}_{st2}")
                pv["b5"][st2] = st_b5

                def em_b6(g, q0=q0):
                    g.tensor_tensor(
                        out=apx(lepf, 0, [(64, 16), (8, 8), (1, 8)]),
                        in0=apx(qkv_s, q0 + 32, [(384, 16), (48, 8), (1, 8)]),
                        in1=apx(lep1, 8, [(0, 16), (0, 8), (1, 8)]), op=ALU.mult)
                    pscr_o = apx(lepf, 1024, [(64, 16), (8, 7), (1, 8)])
                    g.tensor_tensor(
                        out=pscr_o,
                        in0=apx(qkv_s, q0 + 32, [(384, 16), (48, 7), (1, 8)]),
                        in1=apx(lep1, 0, [(0, 16), (0, 7), (1, 8)]), op=ALU.mult)
                    g.tensor_tensor(
                        out=apx(lepf, 8, [(64, 16), (8, 7), (1, 8)]),
                        in0=apx(lepf, 8, [(64, 16), (8, 7), (1, 8)]),
                        in1=pscr_o, op=ALU.add)
                    g.tensor_tensor(
                        out=pscr_o,
                        in0=apx(qkv_s, q0 + 32 + 48, [(384, 16), (48, 7), (1, 8)]),
                        in1=apx(lep1, 16, [(0, 16), (0, 7), (1, 8)]), op=ALU.mult)
                    g.tensor_tensor(
                        out=apx(lepf, 0, [(64, 16), (8, 7), (1, 8)]),
                        in0=apx(lepf, 0, [(64, 16), (8, 7), (1, 8)]),
                        in1=pscr_o, op=ALU.add)
                    return g.tensor_tensor(
                        out=apx(lepf, 0, [(64, 16), (8, 8), (1, 8)]),
                        in0=apx(lepf, 0, [(64, 16), (8, 8), (1, 8)]),
                        in1=apx(lep1, 24, [(0, 16), (0, 8), (1, 8)]), op=ALU.add)
                st_b6 = p.step("gpsimd", em_b6, [st_qkv_done, st_c1, st_b4], tag=f"a0_b1l{ch}_{st2}")
                pv["b6"][st2] = st_b6

                def em_b7(v, st2=st2):
                    return v.tensor_tensor(
                        out=apx(catb, st2 * 2048, [(128, 16), (16, 8), (1, 8)]),
                        in0=apx(S1, 0, [(64, 16), (8, 8), (1, 8)]),
                        in1=apx(lepf, 0, [(64, 16), (8, 8), (1, 8)]), op=ALU.add)
                st_b7 = p.step("vector", em_b7, [st_b5, st_b6, pv["d1tr"] if st2 == 0 else None],
                               tag=f"a0_b1c{ch}_{st2}")
                pv["b7"][st2] = st_b7

            if "catb1" in dbg and ch == dbgch:
                p.dma("sync", lambda sy: [(sy.dma_start(out=dbg["catb1"], in_=catb[:, 0:2048]), 16)],
                      [pv["b7"][0]], tag="a0_dbgb1")
            if 'C2' not in stages:
                continue
            wmbufs = (q2wm, k2wm, v2wm)
            it = 0
            last_te = None
            for st2 in range(2):
                for ty in range(3):
                    for cg in range(2):
                        slot = it % 2
                        boff = slot * 1024

                        def em_tr(t, st2=st2, ty=ty, cg=cg, boff=boff):
                            last = None
                            for ci in range(4):
                                cc = ty * 8 + cg * 4 + ci
                                last = t.transpose(
                                    apx(ps_t_bf, boff + ci * 128, [(1, 128)]),
                                    qkv2_s[:, st2 * 3072 + cc * 128: st2 * 3072 + cc * 128 + 128],
                                    id_bf[:, :])
                            return last
                        trdeps = [st_qkv_done, st_c2, tslot[slot]]
                        if it == 0:
                            trdeps.append(pv["wm_read"])
                        st_tr = p.step("tensor", em_tr, trdeps, tag=f"a0_c2t{ch}_{it}")
                        pv["c2tr_last"] = st_tr

                        def em_te(v, st2=st2, ty=ty, cg=cg, boff=boff):
                            return ecopy(v,
                                out=apx(wmbufs[ty], st2 * 1024 + cg * 4, [(8, 128), (1, 4)]),
                                in_=apx(ps_t_bf, boff, [(1, 128), (128, 4)]))
                        st_te = p.step("scalar", em_te, [st_tr], tag=f"a0_c2e{ch}_{it}")
                        tslot[slot] = st_te
                        last_te = st_te
                        it += 1
            st_wm_done = last_te

            if 'C3' not in stages:
                continue
            c3_deps = [st_wm_done, pv["b5"][1]]
            st_s2red = None
            for rnd in range(2):
                def em_p(v, rnd=rnd):
                    last = None
                    for el in range(2):
                        e = rnd * 2 + el
                        for j in range(8):
                            last = v.tensor_tensor(
                                out=apx(prodb, el * 4096 + 8 * j, [(512, 8), (64, 8), (1, 8)]),
                                in0=apx(q2wm, 8 * e, [(256, 8), (32, 8), (1, 8)]),
                                in1=apx(k2wm, 8 * e + 32 * j, [(256, 8), (0, 8), (1, 8)]),
                                op=ALU.mult)
                    return last
                st_p = p.step("vector", em_p, c3_deps, tag=f"a0_c3p{ch}_{rnd}")

                def em_r(v, rnd=rnd):
                    last = None
                    for el in range(2):
                        e = rnd * 2 + el
                        last = v.tensor_reduce(
                            out=apx(S1, e * 512, [(1, 512)]),
                            in_=apx(prodb, el * 4096, [(8, 512), (1, 8)]),
                            axis=AX.X, op=ALU.add)
                    return last
                st_s2red = p.step("vector", em_r, [st_p], tag=f"a0_c3r{ch}_{rnd}")

            def em_s2m(v):
                v.reduce_max(out=apx(mx, 0, [(1, 256)]),
                             in_=apx(S1, 0, [(8, 256), (1, 8)]), axis=AX.X)
                return v.tensor_tensor(
                    out=apx(S1, 0, [(8, 256), (1, 8)]),
                    in0=apx(S1, 0, [(8, 256), (1, 8)]),
                    in1=apx(mx, 0, [(1, 256), (0, 8)]), op=ALU.subtract)
            st_s2m = p.step("vector", em_s2m, [st_s2red], tag=f"a0_c3m{ch}")

            st_s2e = p.step("scalar",
                            lambda s: s.activation(out=Pb[:, 0:2048], in_=S1[:, 0:2048],
                                                   func=AF.Exp, scale=SQ),
                            [st_s2m], tag=f"a0_c3e{ch}")

            def em_s2n(v):
                v.tensor_reduce(out=apx(den, 0, [(1, 256)]),
                                in_=apx(Pb, 0, [(8, 256), (1, 8)]), axis=AX.X, op=ALU.add)
                v.reciprocal(out=den[:, 0:256], in_=den[:, 0:256])
                return v.tensor_tensor(
                    out=apx(Pb, 0, [(8, 256), (1, 8)]),
                    in0=apx(Pb, 0, [(8, 256), (1, 8)]),
                    in1=apx(den, 0, [(1, 256), (0, 8)]), op=ALU.mult)
            st_s2n = p.step("vector", em_s2n, [st_s2e], tag=f"a0_c3n{ch}")

            o2red = []
            for rnd in range(2):
                def em_op(v, rnd=rnd):
                    last = None
                    for el in range(2):
                        e = rnd * 2 + el
                        for c in range(8):
                            last = v.tensor_tensor(
                                out=apx(prodb, el * 4096 + 8 * c, [(512, 8), (64, 8), (1, 8)]),
                                in0=apx(Pb, e * 512, [(64, 8), (8, 8), (1, 8)]),
                                in1=apx(v2wm, 8 * e + c, [(256, 8), (0, 8), (32, 8)]),
                                op=ALU.mult)
                    return last
                opdeps = [st_s2n]
                if rnd == 0:
                    opdeps.append(pv["c4ev"][0])
                st_op = p.step("vector", em_op, opdeps, tag=f"a0_c3o{ch}_{rnd}")

                def em_or(v, rnd=rnd):
                    last = None
                    for el in range(2):
                        e = rnd * 2 + el
                        last = v.tensor_reduce(
                            out=apx(o2cp, e, [(32, 8), (4, 8), (256, 8)]),
                            in_=apx(prodb, el * 4096, [(8, 512), (1, 8)]),
                            axis=AX.X, op=ALU.add)
                    return last
                st_or = p.step("vector", em_or, [st_op], tag=f"a0_c3or{ch}_{rnd}")
                o2red.append(st_or)
            pv["c3red_last"] = o2red[-1]

            def em_l2a(g):
                last = None
                for e in range(4):
                    g.tensor_tensor(
                        out=apx(lepf, 64 * e, [(256, 8), (8, 8), (1, 8)]),
                        in0=apx(v2wm, 8 * e, [(256, 8), (32, 8), (1, 8)]),
                        in1=apx(lep2, 8, [(0, 8), (0, 8), (1, 8)]), op=ALU.mult)
                    g.tensor_tensor(
                        out=apx(lepf, 64 * e, [(256, 8), (8, 8), (1, 8)]),
                        in0=apx(lepf, 64 * e, [(256, 8), (8, 8), (1, 8)]),
                        in1=apx(lep2, 24, [(0, 8), (0, 8), (1, 8)]), op=ALU.add)
                    g.tensor_tensor(
                        out=apx(S1, 0, [(256, 8), (8, 7), (1, 8)]),
                        in0=apx(v2wm, 8 * e, [(256, 8), (32, 7), (1, 8)]),
                        in1=apx(lep2, 0, [(0, 8), (0, 7), (1, 8)]), op=ALU.mult)
                    g.tensor_tensor(
                        out=apx(lepf, 64 * e + 8, [(256, 8), (8, 7), (1, 8)]),
                        in0=apx(lepf, 64 * e + 8, [(256, 8), (8, 7), (1, 8)]),
                        in1=apx(S1, 0, [(256, 8), (8, 7), (1, 8)]), op=ALU.add)
                    g.tensor_tensor(
                        out=apx(S1, 0, [(256, 8), (8, 7), (1, 8)]),
                        in0=apx(v2wm, 8 * e + 32, [(256, 8), (32, 7), (1, 8)]),
                        in1=apx(lep2, 16, [(0, 8), (0, 7), (1, 8)]), op=ALU.mult)
                    last = g.tensor_tensor(
                        out=apx(lepf, 64 * e, [(256, 8), (8, 7), (1, 8)]),
                        in0=apx(lepf, 64 * e, [(256, 8), (8, 7), (1, 8)]),
                        in1=apx(S1, 0, [(256, 8), (8, 7), (1, 8)]), op=ALU.add)
                return last
            st_l2 = p.step("gpsimd", em_l2a, [st_wm_done, pv["b6"][1], st_s2n, pv["c3red_last"], pv["b7"][1]],
                           tag=f"a0_c3l{ch}")

            def em_l2add(v):
                last = None
                for e in range(4):
                    last = v.tensor_tensor(
                        out=apx(o2cp, e, [(32, 8), (4, 8), (256, 8)]),
                        in0=apx(o2cp, e, [(32, 8), (4, 8), (256, 8)]),
                        in1=apx(lepf, 64 * e, [(256, 8), (8, 8), (1, 8)]), op=ALU.add)
                return last
            st_l2a = p.step("vector", em_l2add, [st_l2, pv["c3red_last"]], tag=f"a0_c3la{ch}")
            pv["c3_last"] = st_l2a

            if 'C4' not in stages:
                continue
            c4_last = None
            for st2 in range(2):
                for cpr in range(4):
                    slot = it % 2
                    boff4 = slot * 512

                    def em_t4(t, st2=st2, cpr=cpr, boff4=boff4):
                        last = None
                        for ci in range(2):
                            c = cpr * 2 + ci
                            last = t.transpose(
                                apx(ps_t, boff4 + ci * 128, [(1, 128)]),
                                o2cp[:, 256 * c + st2 * 128: 256 * c + st2 * 128 + 128],
                                id_f[:, :])
                        return last
                    st_t4 = p.step("tensor", em_t4, [st_l2a, tslot[slot], st_c2],
                                   tag=f"a0_c4t{ch}_{st2}_{cpr}")

                    def em_e4(s, st2=st2, cpr=cpr, boff4=boff4):
                        return s.activation(
                            out=apx(catb, st2 * 2048 + 8 + cpr * 2, [(1, 2), (128, 16), (16, 8)]),
                            in_=apx(ps_t, boff4, [(128, 2), (8, 16), (1, 8)]),
                            func=AF.Copy)
                    st_e4 = p.step("scalar", em_e4, [st_t4, pv["b7"][st2], pv["d1tr"]],
                                   tag=f"a0_c4e{ch}_{st2}_{cpr}")
                    tslot[slot] = st_e4
                    c4_last = st_e4
                    it += 1
            pv["c4ev"][0] = c4_last

            if "cat" in dbg and ch == dbgch:
                p.dma("sync", lambda sy: [(sy.dma_start(out=dbg["cat"], in_=catb[:, 0:2048]), 16)],
                      [c4_last, pv["b7"][0]], tag="a0_dbgc")

            if 'D' not in stages:
                continue
            d1_evs = []
            for st2 in range(2):
                for dgr in range(4):
                    slot = it % 2
                    boffb = slot * 1024

                    def em_d1(t, st2=st2, dgr=dgr, boffb=boffb):
                        last = None
                        for di in range(4):
                            dt = dgr * 4 + di
                            last = t.transpose(
                                apx(ps_t_bf, boffb + di * 128, [(1, 128)]),
                                catb[:, st2 * 2048 + dt * 128: st2 * 2048 + dt * 128 + 128],
                                id_bf[:, :])
                        return last
                    d1deps = [pv["b7"][st2], c4_last, tslot[slot]]
                    if st2 == 0 and dgr == 0:
                        d1deps.append(pv["catT_read"])
                    st_d1 = p.step("tensor", em_d1, d1deps, tag=f"a0_d1t{ch}_{st2}_{dgr}")
                    pv["d1tr"] = st_d1

                    def em_d1e(v, st2=st2, dgr=dgr, boffb=boffb):
                        return v.tensor_copy(
                            out=apx(catT, dgr * 1024 + st2 * 128, [(256, 4), (1, 128)]),
                            in_=apx(ps_t_bf, boffb, [(128, 4), (1, 128)]))
                    st_d1e = p.step("vector", em_d1e, [st_d1, pv["catT_read"]], tag=f"a0_d1e{ch}_{st2}_{dgr}")
                    tslot[slot] = st_d1e
                    d1_evs.append(st_d1e)
                    it += 1
            st_d1_done = d1_evs[-1]

            d3_last_ev = None
            for dpr in range(8):
                slot = it % 2
                def em_d2(t, dpr=dpr):
                    last = None
                    for di in range(2):
                        dt = dpr * 2 + di
                        last = t.matmul(apx(ps_a, di * 512, [(1, 256)]),
                                        projbd[:, :],
                                        catT[:, dt * 256:(dt + 1) * 256],
                                        start=True, stop=True)
                    return last
                d2deps = [st_d1_done, st_c3, pv["qkv_ev"][0]]
                if dpr > 1:
                    d2deps.append(pv["d2ev2"] if False else None)
                st_d2 = p.step("tensor", em_d2,
                               d2deps + [pv.get("d2_evprev")], tag=f"a0_d2m{ch}_{dpr}")
                pv["qkv_mm"][0] = st_d2

                def em_d2e(s, dpr=dpr):
                    return s.activation(out=attT[:, 0:512] if False else attT[:, :512],
                                        in_=apx(ps_a, 0, [(512, 2), (1, 256)]), func=AF.Copy)
                st_d2e = p.step("scalar", em_d2e, [st_d2, pv.get("d3_trprev")],
                                tag=f"a0_d2e{ch}_{dpr}")
                pv["qkv_ev"][0] = st_d2e
                pv["d2_evprev"] = st_d2e

                def em_d3(t, dpr=dpr, slot=slot):
                    last = None
                    for di in range(2):
                        for st2 in range(2):
                            last = t.transpose(
                                apx(ps_t_bf, slot * 1024 + (di * 2 + st2) * 128, [(1, 128)]),
                                attT[:, di * 256 + st2 * 128: di * 256 + st2 * 128 + 128],
                                id_bf[:, :])
                    return last
                st_d3 = p.step("tensor", em_d3, [st_d2e, tslot[slot]], tag=f"a0_d3t{ch}_{dpr}")
                pv["d3_trprev"] = st_d3
                pv["d3tr"] = st_d3

                def em_d3e(v, dpr=dpr, slot=slot):
                    last = None
                    for st2 in range(2):
                        ddeps = None
                        last = v.tensor_tensor(
                            out=apx(attb, st2 * 2048 + dpr * 256, [(128, 2), (16, 8), (1, 16)]),
                            in0=apx(ps_t_bf, slot * 1024 + st2 * 128, [(256, 2), (16, 8), (1, 16)]),
                            in1=apx(pb_r, 0, [(0, 2), (0, 8), (1, 16)]), op=ALU.add)
                    return last
                d3edeps = [st_d3, st_c1]
                if dpr == 0:
                    d3edeps += [pv["a1"][0], pv["a1"][1]]
                st_d3e = p.step("vector", em_d3e, d3edeps, tag=f"a0_d3e{ch}_{dpr}")
                tslot[slot] = st_d3e
                d3_last_ev = st_d3e
                it += 1
            st_att_done = d3_last_ev

            if "att" in dbg and ch == dbgch:
                p.dma("sync", lambda sy: [(sy.dma_start(out=dbg["att"],
                      in_=attb[:, dbgst2 * 2048:dbgst2 * 2048 + 2048]), 16)],
                      [st_att_done], tag="a0_dbga")

            if 'E' not in stages:
                continue
            for st2 in range(2):
                e_steps = []
                for dgr in range(4):
                    def em_e(t, dgr=dgr, st2=st2):
                        last = None
                        for di in range(4):
                            dt = dgr * 4 + di
                            last = t.transpose(
                                apx(ps_u_bf, dt * 128, [(1, 128)]),
                                xnc[:, dt * SCS + st2 * 128: dt * SCS + st2 * 128 + 128],
                                id_bf[:, :])
                        return last
                    edeps = [st_xl, st_c2]
                    if dgr == 0:
                        edeps.append(pv["a5"][st2] if st2 == 1 or ch > 0 else None)
                        edeps.append(pv["a5"][1 - st2])
                    st_e = p.step("tensor", em_e, edeps, tag=f"a0_et{ch}_{st2}_{dgr}")
                    e_steps.append(st_e)
                st_e_done = e_steps[-1]
                pv["e_last"] = st_e_done

                a0 = st2 * 2048

                st_sq = p.step("scalar",
                               lambda s, a0=a0: s.activation(out=sqb[:, :], in_=attb[:, a0:a0 + 2048],
                                                             func=AF.Square),
                               [st_att_done, pv["fred"], st_e_done], tag=f"a0_fsq{ch}_{st2}")

                def em_red(v, a0=a0):
                    v.tensor_reduce(out=apx(ms, 0, [(8, 16), (1, 8)]),
                                    in_=apx(attb, a0, [(128, 16), (16, 8), (1, 16)]),
                                    axis=AX.X, op=ALU.add)
                    return v.tensor_reduce(out=apx(m2, 0, [(8, 16), (1, 8)]),
                                           in_=apx(sqb, 0, [(128, 16), (16, 8), (1, 16)]),
                                           axis=AX.X, op=ALU.add)
                st_red = p.step("vector", em_red, [st_sq, st_att_done], tag=f"a0_fred{ch}_{st2}")
                pv["fred"] = st_red

                def em_stat(v):
                    v.tensor_scalar(out=ms[:, :], in0=ms[:, :], scalar1=1.0 / 16, scalar2=None,
                                    op0=ALU.mult)
                    v.tensor_tensor(out=den[:, 0:128], in0=ms[:, :], in1=ms[:, :], op=ALU.mult)
                    return v.scalar_tensor_tensor(out=m2[:, :], in0=m2[:, :], scalar=1.0 / 16,
                                                  in1=den[:, 0:128], op0=ALU.mult,
                                                  op1=ALU.subtract)
                st_stat = p.step("vector", em_stat, [st_red], tag=f"a0_fst{ch}_{st2}")

                st_sqrt = p.step("scalar",
                                 lambda s: s.activation(out=m2[:, :], in_=m2[:, :], func=AF.Sqrt,
                                                        bias=eps_c[:, :]),
                                 [st_stat, st_c2], tag=f"a0_fsr{ch}_{st2}")
                st_rc = p.step("vector", lambda v: v.reciprocal(out=m2[:, :], in_=m2[:, :]),
                               [st_sqrt], tag=f"a0_frc{ch}_{st2}")

                def em_ap(v, a0=a0, st2=st2):
                    v.tensor_tensor(out=apx(fscr, 0, [(128, 16), (16, 8), (1, 16)]),
                                    in0=apx(attb, a0, [(128, 16), (16, 8), (1, 16)]),
                                    in1=apx(ms, 0, [(8, 16), (1, 8), (0, 16)]), op=ALU.subtract)
                    v.tensor_tensor(out=apx(fscr, 0, [(128, 16), (16, 8), (1, 16)]),
                                    in0=apx(fscr, 0, [(128, 16), (16, 8), (1, 16)]),
                                    in1=apx(m2, 0, [(8, 16), (1, 8), (0, 16)]), op=ALU.mult)
                    v.tensor_tensor(out=apx(fscr, 0, [(128, 16), (16, 8), (1, 16)]),
                                    in0=apx(fscr, 0, [(128, 16), (16, 8), (1, 16)]),
                                    in1=apx(lnw_r, 0, [(0, 16), (0, 8), (1, 16)]), op=ALU.mult)
                    return v.tensor_tensor(out=apx(fscr, 0, [(128, 16), (16, 8), (1, 16)]),
                                           in0=apx(fscr, 0, [(128, 16), (16, 8), (1, 16)]),
                                           in1=apx(lnb_r, 0, [(0, 16), (0, 8), (1, 16)]),
                                           op=ALU.add)
                st_a1 = p.step("vector", em_ap, [st_rc, st_c1], tag=f"a0_fap{ch}_{st2}")
                pv["a1"][st2] = st_a1

                def em_a5(v, st2=st2):
                    return v.tensor_tensor(
                        out=apx(tokf, st2 * 2048, [(128, 16), (16, 8), (1, 16)]).transpose(
                            [0, 1, 2, 3]) if False else apx(
                            tokf, st2 * 2048, [(8, 16), (1, 8), (128, 16)]),
                        in0=apx(fscr, 0, [(128, 16), (16, 8), (1, 16)]),
                        in1=apx(ps_u_bf, 0, [(128, 16), (16, 8), (1, 16)]), op=ALU.add)
                if "fscr" in dbg and ch == dbgch and st2 == dbgst2:
                    p.dma("sync", lambda sy: [(sy.dma_start(out=dbg["fscr"], in_=fscr[:, :]), 16)],
                          [st_a1], tag="a0_dbgf")
                    def em_xte(v):
                        return v.tensor_copy(out=sqb[:, :], in_=ps_u_bf[:, 0:2048])
                    st_xte = p.step("vector", em_xte, [st_e_done, pv["fred"]], tag="a0_dbgxe")
                    p.dma("sync", lambda sy: [(sy.dma_start(out=dbg["xnt"], in_=sqb[:, :]), 16)],
                          [st_xte], tag="a0_dbgx")
                st_a5 = p.step("vector", em_a5,
                               [st_a1, st_e_done, pv["scat"][st2]], tag=f"a0_fa5{ch}_{st2}")
                pv["a5"][st2] = st_a5

                def em_sc(sy, ch=ch, st2=st2):
                    st = ch * 2 + st2
                    r = []
                    for a in range(8):
                        dst = bass.AP(tensor=ca_dram.tensor,
                                      offset=ca_dram.offset + (8 * st + a) * 2048,
                                      ap=[[128, 16], [262144, 16], [1, 128]])
                        src = apx(tokf, st2 * 2048, [(128, 16), (1, 128)], npart=16, p0=16 * a)
                        r.append((sy.dma_start(out=dst, in_=src), 16))
                    return r
                st_sc = p.dma("sync", em_sc, [st_a5], n=8, tag=f"a0_sc{ch}_{st2}",
                              sem="a0_sc" + str(st2))
                pv["scat"][st2] = st_sc

            pv["wm_read"] = pv["c3_last"]
            pv["catT_read"] = pv["d2_evprev"]
        return p.last_steps()



def build_l1attn_phase(cb, xn_dram, ca_dram, w, deps=(), nheads=16, debug=None):
    nc, p = cb.nc, cb.p
    NJ = 32
    SQ = 128.0 ** -0.5
    with contextlib.ExitStack() as es:
        l = CB(nc, p, es)
        xn1 = l.sb("a1_xn", [128, NDT * 2048], BF16)
        whead = l.sb("a1_wh", [128, 2 * 3 * 2048], BF16)
        fqk = l.sb("a1_fqk", [128, 4 * 128], BF16)
        bq = l.sb("a1_bq", [128, 2], F32)
        bkrep = l.sb("a1_bkr", [128, 2 * 128], F32)
        bkc = l.sb("a1_bkc", [128, 2], F32)
        nwc = l.sb("a1_nw", [128, 1], F32)
        eps_c = l.sb("a1_eps", [128, 1], F32)
        ones_c = l.sb("a1_ones", [128, 1], BF16)
        id_bf = l.sb("a1_idbf", [128, 128], BF16)
        tril = l.sb("a1_tril", [64, 64], BF16)
        qh = l.sb("a1_qh", [128, 2048], BF16)
        kh = l.sb("a1_kh", [128, 2048], BF16)
        vh = l.sb("a1_vh", [128, 2048], BF16)
        q2 = l.sb("a1_q2", [128, 2048], BF16)
        k2 = l.sb("a1_k2", [128, 2048], BF16)
        k2tm = l.sb("a1_k2tm", [128, 2048], BF16)
        vtm = l.sb("a1_vtm", [128, 2048], BF16)
        kvsb = l.sb("a1_kvsb", [128, 128], BF16)
        sT = l.sb("a1_sT", [128, 64], BF16)
        onb = l.sb("a1_onb", [128, 2048], BF16)
        sqs = l.sb("a1_sqs", [128, 2048], BF16)
        msrep = l.sb("a1_ms", [128, 2048], F32)
        t1 = l.sb("a1_t1", [128, 512], F32)
        t2 = l.sb("a1_t2", [128, 512], F32)
        onorm = l.sb("a1_on", [128, 2048], BF16)
        wob = l.sb("a1_wob", [128, 2 * 2048], BF16)
        ow = l.sb("a1_ow", [128, NDT * 512], BF16)
        osb = l.sb("a1_osb", [128, 2 * 512], F32)
        psA = l.ps("a1_psA", [128, 512])
        psB = l.ps("a1_psB", [128, 512])
        psC = l.ps("a1_psC", [128, 512])
        psD = l.ps("a1_psD", [128, 512])
        ps_kv = l.ps("a1_pskv", [128, 512])
        ps_s = l.ps("a1_pss", [64, 512])
        ps_o = l.ps("a1_pso", [128, 512])
        psD_bf = psD.ap().bitcast(BF16)

        onrm_d = l.dram("a1_onrm", [2048, 2048], BF16)
        rows_d = l.dram("a1_rows", [1, 2048])

        def em_c1(g):
            r = []
            for i, nm in enumerate(["fq1T", "fq2T", "fk1T", "fk2T"]):
                r.append((g.dma_start(out=fqk[:, i * 128:(i + 1) * 128], in_=w[nm]), 16))
            r.append((g.dma_start(out=bq[:, 0:1], in_=w["fq1_b"]), 16))
            r.append((g.dma_start(out=bq[:, 1:2], in_=w["fq2_b"]), 16))
            r.append((g.dma_start(out=bkrep[:, 0:128],
                                  in_=w["fk1_b"].unsqueeze(0).broadcast_to([128, 128])), 16))
            r.append((g.dma_start(out=bkrep[:, 128:256],
                                  in_=w["fk2_b"].unsqueeze(0).broadcast_to([128, 128])), 16))
            r.append((g.dma_start(out=bkc[:, 0:1], in_=w["fk1_b"].unsqueeze(1)), 16))
            r.append((g.dma_start(out=bkc[:, 1:2], in_=w["fk2_b"].unsqueeze(1)), 16))
            r.append((g.dma_start(out=nwc[:, :], in_=w["norm_w"]), 16))
            return r
        st_c1 = p.dma("gpsimd", em_c1, deps, n=11, tag="a1_c1", sem="a1_c")

        def em_c2(g):
            g.memset(eps_c[:, :], 1e-5)
            g.memset(ones_c[:, :], 1.0)
            g.memset(id_bf[:, :], 1.0)
            g.affine_select(out=id_bf[:, :], in_=id_bf[:, :], pattern=[[-1, 128]],
                            compare_op=ALU.is_equal, fill=0.0, base=0, channel_multiplier=1)
            g.memset(tril[:, :], 1.0)
            return g.affine_select(out=tril[:, :], in_=tril[:, :], pattern=[[1, 64]],
                                   compare_op=ALU.is_ge, fill=0.0, base=0,
                                   channel_multiplier=-1)
        st_c2 = p.step("gpsimd", em_c2, deps, tag="a1_c2")

        xn_r = xn_dram.rearrange("(dt p) s -> p dt s", p=128)
        st_xn = p.dma("gpsimd", lambda g: [(g.dma_start(out=xn1[:, :], in_=xn_r[:, :, :]), 16)],
                      deps, tag="a1_xl", sem="a1_xn")

        pv = dict(wl=[None, None], A=None, B=None, C=None, D=None, S=None,
                  kvu=None, onbu=None, onormu=None, rap=None, rrep=None,
                  chunk_last=None)

        for h in range(nheads):
            hb = h % 2
            woff = hb * 3 * 2048

            def em_wl(g, h=h, woff=woff):
                return [(g.dma_start(out=whead[:, woff + i * 2048: woff + (i + 1) * 2048],
                                     in_=w[nm][:, h, :]), 16)
                        for i, nm in enumerate(["wq_pm", "wk_pm", "wv_pm"])]
            st_wl = p.dma("gpsimd", em_wl, list(deps) + [pv["wl"][hb]], n=3, tag=f"a1_wl{h}",
                          sem="a1_wl" + str(hb))

            proj_last = None
            for ti, (tgt, base) in enumerate([(qh, 0), (kh, 2048), (vh, 4096)]):
                for sc in range(4):
                    pb = (psA, psB)[sc % 2]
                    pk = ("A", "B")[sc % 2]

                    def em_pm(t, sc=sc, base=base, woff=woff, pb=pb):
                        last = None
                        for dt in range(NDT):
                            last = t.matmul(pb[:, :],
                                            apx(whead, woff + base + dt * 128, [(1, 128)]),
                                            xn1[:, dt * 2048 + sc * 512: dt * 2048 + (sc + 1) * 512],
                                            start=(dt == 0), stop=(dt == NDT - 1))
                        return last
                    st_pm = p.step("tensor", em_pm, [st_wl, st_xn, pv[pk]],
                                   tag=f"a1_pm{h}_{ti}_{sc}")

                    def em_pe(s, tgt=tgt, sc=sc, pb=pb):
                        return s.activation(out=tgt[:, sc * 512:(sc + 1) * 512],
                                            in_=pb[:, :], func=AF.Copy)
                    st_pe = p.step("scalar", em_pe, [st_pm], tag=f"a1_pe{h}_{ti}_{sc}")
                    pv[pk] = st_pe
                    proj_last = st_pe
            pv["wl"][hb] = proj_last
            st_proj = proj_last

            st_fm = None
            for fi, (src, dst) in enumerate([(qh, q2), (kh, k2)]):
                for sc in range(4):
                    def em_f1(t, fi=fi, sc=sc, src=src):
                        t.matmul(psA[:, :], fqk[:, (2 * fi) * 128:(2 * fi + 1) * 128],
                                 src[:, sc * 512:(sc + 1) * 512], start=True, stop=True)
                        return t.matmul(psB[:, :], fqk[:, (2 * fi + 1) * 128:(2 * fi + 2) * 128],
                                        src[:, sc * 512:(sc + 1) * 512], start=True, stop=True)
                    st_f1 = p.step("tensor", em_f1, [st_proj, st_c1, pv["A"], pv["B"]],
                                   tag=f"a1_f{h}_{fi}_{sc}")

                    if fi == 0:
                        def em_fe(v, sc=sc, dst=dst):
                            v.tensor_scalar(out=t1[:, :], in0=psA[:, :], scalar1=bq[:, 0:1],
                                            scalar2=SQ, op0=ALU.add, op1=ALU.mult)
                            return v.scalar_tensor_tensor(out=dst[:, sc * 512:(sc + 1) * 512],
                                                          in0=psB[:, :], scalar=bq[:, 1:2],
                                                          in1=t1[:, :], op0=ALU.add,
                                                          op1=ALU.mult)
                    else:
                        def em_fe(v, sc=sc, dst=dst):
                            v.tensor_scalar(out=t1[:, :], in0=psA[:, :], scalar1=bkc[:, 0:1],
                                            scalar2=None, op0=ALU.add)
                            return v.scalar_tensor_tensor(out=dst[:, sc * 512:(sc + 1) * 512],
                                                          in0=psB[:, :], scalar=bkc[:, 1:2],
                                                          in1=t1[:, :], op0=ALU.add,
                                                          op1=ALU.mult)
                    fdeps = [st_f1, st_c1]
                    if fi == 0 and sc == 0:
                        fdeps.append(pv["chunk_last"])
                    st_fe = p.step("vector", em_fe, fdeps, tag=f"a1_fe{h}_{fi}_{sc}")
                    pv["A"] = st_fe
                    pv["B"] = st_fe
                    st_fm = st_fe

            k2_done = None
            for grp in range(4):
                def em_k3(t, grp=grp):
                    last = None
                    for i in range(4):
                        cb4 = grp * 4 + i
                        t.matmul(psC[:, i * 128:(i + 1) * 128],
                                 kh[:, cb4 * 128:(cb4 + 1) * 128],
                                 fqk[:, 2 * 128:3 * 128], start=(i == 0), stop=(i == 3),
                                 skip_group_check=True)
                        last = t.matmul(psD[:, i * 128:(i + 1) * 128],
                                        kh[:, cb4 * 128:(cb4 + 1) * 128],
                                        fqk[:, 3 * 128:4 * 128], start=(i == 0), stop=(i == 3),
                                        skip_group_check=True)
                    return last
                st_k3 = p.step("tensor", em_k3, [st_proj, st_c1, pv["C"], pv["D"]],
                               tag=f"a1_k3{h}_{grp}")

                def em_k3e(v, grp=grp):
                    v.tensor_tensor(out=t1[:, :], in0=psC[:, :],
                                    in1=apx(bkrep, 0, [(0, 4), (1, 128)]), op=ALU.add)
                    v.tensor_tensor(out=t2[:, :], in0=psD[:, :],
                                    in1=apx(bkrep, 128, [(0, 4), (1, 128)]), op=ALU.add)
                    return v.tensor_tensor(out=k2tm[:, grp * 512:(grp + 1) * 512],
                                           in0=t1[:, :], in1=t2[:, :], op=ALU.mult)
                kdeps = [st_k3, st_c1]
                if grp == 0:
                    kdeps.append(pv["chunk_last"])
                st_k3e = p.step("vector", em_k3e, kdeps, tag=f"a1_k3e{h}_{grp}")
                pv["C"] = st_k3e
                pv["D"] = st_k3e
                k2_done = st_k3e

            vtm_done = None
            for grp in range(4):
                def em_vt(t, grp=grp):
                    last = None
                    for i in range(4):
                        cb4 = grp * 4 + i
                        last = t.transpose(apx(psD_bf, i * 128, [(1, 128)]),
                                           vh[:, cb4 * 128:(cb4 + 1) * 128], id_bf[:, :])
                    return last
                st_vt = p.step("tensor", em_vt, [st_proj, st_c2, pv["D"]],
                               tag=f"a1_vt{h}_{grp}")

                def em_vte(s, grp=grp):
                    return s.activation(out=vtm[:, grp * 512:(grp + 1) * 512],
                                        in_=apx(psD_bf, 0, [(1, 512)]), func=AF.Copy)
                vdeps = [st_vt]
                if grp == 0:
                    vdeps.append(pv["chunk_last"])
                st_vte = p.step("scalar", em_vte, vdeps, tag=f"a1_vte{h}_{grp}")
                pv["D"] = st_vte
                vtm_done = st_vte

            st_kvev = None
            st_kvmm = None
            last_oev = pv["onbu"]
            last_omm = None
            last_sev = pv["S"]
            last_smm = None
            for j in range(NJ):
                cb64 = (j % 2) * 64
                ck = j // 2

                if j > 0:
                    def em_kve(s, j=j):
                        return s.activation(out=kvsb[:, :], in_=ps_kv[:, 0:128], func=AF.Copy)
                    st_kvev = p.step("scalar", em_kve, [st_kvmm, last_omm],
                                     tag=f"a1_kve{h}_{j}")

                def em_kv(t, j=j, cb64=cb64, ck=ck):
                    return t.matmul(ps_kv[:, 0:128],
                                    apx(k2tm, ck * 128, [(1, 128)], npart=64, p0=cb64),
                                    apx(vtm, ck * 128, [(1, 128)], npart=64, p0=cb64),
                                    start=(j == 0), stop=(j == NJ - 1))
                kvdeps = [k2_done, vtm_done, st_kvev]
                if j == 0:
                    kvdeps += [pv["kvu"], st_fm]
                st_kvmm = p.step("tensor", em_kv, kvdeps, tag=f"a1_kv{h}_{j}")

                def em_s(t, j=j):
                    return t.matmul(ps_s[:, 0:64], k2[:, j * 64:(j + 1) * 64],
                                    q2[:, j * 64:(j + 1) * 64], start=True, stop=True)
                sdeps = [st_fm, k2_done, last_sev]
                st_smm = p.step("tensor", em_s, sdeps, tag=f"a1_s{h}_{j}")

                def em_se(v, j=j, cb64=cb64):
                    return v.tensor_tensor(out=sT[cb64:cb64 + 64, :], in0=ps_s[:, 0:64],
                                           in1=tril[:, :], op=ALU.mult)
                last_sev = p.step("vector", em_se, [st_smm, st_c2], tag=f"a1_se{h}_{j}")

                def em_o(t, j=j, cb64=cb64, ck=ck):
                    if j > 0:
                        t.matmul(ps_o[:, 0:64], kvsb[:, :], q2[:, j * 64:(j + 1) * 64],
                                 start=True, stop=False)
                    return t.matmul(ps_o[:, 0:64],
                                    apx(vtm, ck * 128, [(1, 128)], npart=64, p0=cb64),
                                    sT[cb64:cb64 + 64, :], start=(j == 0), stop=True)
                st_omm = p.step("tensor", em_o, [last_sev, st_kvev, last_oev],
                                tag=f"a1_o{h}_{j}")
                last_omm = st_omm

                def em_oe(s, j=j):
                    return s.activation(out=onb[:, j * 64:(j + 1) * 64], in_=ps_o[:, 0:64],
                                        func=AF.Copy)
                odeps = [st_omm]
                if j == 0:
                    odeps.append(pv["rap"])
                last_oev = p.step("scalar", em_oe, odeps, tag=f"a1_oe{h}_{j}")
            pv["kvu"] = last_omm
            pv["chunk_last"] = last_omm
            pv["S"] = last_sev

            st_sq = p.step("scalar",
                           lambda s: s.activation(out=sqs[:, :], in_=onb[:, :], func=AF.Square),
                           [last_oev, pv["rrep"]], tag=f"a1_rsq{h}")

            def em_rmm(t):
                last = None
                for pb, sc in [(psA, 0), (psB, 1), (psC, 2), (ps_s, 3)]:
                    last = t.matmul(pb[0:1, 0:512], ones_c[:, :],
                                    sqs[:, sc * 512:(sc + 1) * 512], start=True, stop=True)
                return last
            st_rmm = p.step("tensor", em_rmm,
                            [st_sq, pv["A"], pv["B"], pv["C"], last_sev, st_c2],
                            tag=f"a1_rmm{h}")

            def em_rrow(v):
                v.tensor_copy(out=msrep[0:1, 0:512], in_=psA[0:1, 0:512])
                v.tensor_copy(out=msrep[0:1, 512:1024], in_=psB[0:1, 0:512])
                v.tensor_copy(out=msrep[0:1, 1024:1536], in_=psC[0:1, 0:512])
                return v.tensor_copy(out=msrep[0:1, 1536:2048], in_=ps_s[0:1, 0:512])
            st_rrow = p.step("vector", em_rrow, [st_rmm], tag=f"a1_rrow{h}")
            pv["A"] = st_rrow
            pv["B"] = st_rrow
            pv["C"] = st_rrow
            pv["S"] = st_rrow

            st_rout = p.dma("gpsimd",
                            lambda g: [(g.dma_start(out=rows_d[0:1, :], in_=msrep[0:1, :]), 16)],
                            [st_rrow], tag=f"a1_rout{h}", sem="a1_rms")
            st_rrep = p.dma("gpsimd",
                            lambda g: [(g.dma_start(out=msrep[:, :],
                                                    in_=rows_d[0:1, :].broadcast_to([128, 2048])), 16)],
                            [st_rout, pv["rap"]], tag=f"a1_rrep{h}", sem="a1_rms")
            pv["rrep"] = st_rrep

            def em_rtf(s):
                s.activation(out=msrep[:, :], in_=msrep[:, :], func=AF.Ln,
                             bias=eps_c[:, :], scale=1.0 / 128)
                return s.activation(out=msrep[:, :], in_=msrep[:, :], func=AF.Exp, scale=-0.5)
            st_rtf = p.step("scalar", em_rtf, [st_rrep, st_c2], tag=f"a1_rtf{h}")

            def em_rap(v):
                return v.scalar_tensor_tensor(out=onorm[:, :], in0=onb[:, :],
                                              scalar=nwc[:, :], in1=msrep[:, :],
                                              op0=ALU.mult, op1=ALU.mult)
            st_rap = p.step("vector", em_rap, [st_rtf, st_c1, pv["onormu"]], tag=f"a1_rap{h}")
            pv["rap"] = st_rap
            pv["onbu"] = st_rap

            def em_ost(sy, h=h):
                return [(sy.dma_start(out=onrm_d[h * 128:(h + 1) * 128, :], in_=onorm[:, :]), 16)]
            st_ost = p.dma("sync", em_ost, [st_rap], tag=f"a1_ost{h}", sem="a1_os")
            pv["onormu"] = st_ost

        if debug is not None and "sT" in debug:
            p.dma("sync", lambda sy: [(sy.dma_start(out=debug["sT"], in_=sT[:, :]), 16)],
                  [pv["S"], pv["chunk_last"]], tag="a1_dbgsT")
            p.dma("sync", lambda sy: [(sy.dma_start(out=debug["kvsb"], in_=kvsb[:, :]), 16)],
                  [pv["chunk_last"]], tag="a1_dbgkv")
        if debug is not None and "onb" in debug:
            p.dma("sync", lambda sy: [(sy.dma_start(out=debug["onb"], in_=onb[:, :]), 16)],
                  [pv["onbu"], pv["kvu"]], tag="a1_dbgob")
            p.dma("sync", lambda sy: [(sy.dma_start(out=debug["q2"], in_=q2[:, :]), 16)],
                  [pv["chunk_last"]], tag="a1_dbgq2")
            p.dma("sync", lambda sy: [(sy.dma_start(out=debug["k2tm"], in_=k2tm[:, :]), 16)],
                  [pv["chunk_last"]], tag="a1_dbgk2")
        if debug is not None and "onrm" in debug:
            p.dma("sync", lambda sy: [(sy.dma_start(out=debug["onrm"], in_=onrm_d[:, :]), 16)],
                  [pv["onormu"]], tag="a1_dbgon")

        on_r = onrm_d.rearrange("(dt p) s -> p dt s", p=128)
        last_owin_mm = None
        wo_mm = [None, None]
        wo_ev = [None, None]
        for sc in range(4):
            def em_owl(g, sc=sc):
                return [(g.dma_start(out=ow[:, :], in_=on_r[:, :, sc * 512:(sc + 1) * 512]), 16)]
            st_owl = p.dma("gpsimd", em_owl, [pv["onormu"], last_owin_mm], tag=f"a1_owl{sc}",
                           sem="a1_ow")

            for et in range(NDT):
                eb = et % 2

                def em_wol(g, et=et, eb=eb):
                    return [(g.dma_start(out=wob[:, eb * 2048:(eb + 1) * 2048],
                                         in_=w["wo_pm"][:, et, :]), 16)]
                st_wol = p.dma("gpsimd", em_wol, [wo_mm[eb]], tag=f"a1_wol{sc}_{et}",
                               sem="a1_wo" + str(eb))

                def em_wom(t, et=et, eb=eb):
                    pb = (psA, psB)[eb]
                    last = None
                    for dt in range(NDT):
                        last = t.matmul(pb[:, :],
                                        apx(wob, eb * 2048 + dt * 128, [(1, 128)]),
                                        ow[:, dt * 512:(dt + 1) * 512],
                                        start=(dt == 0), stop=(dt == NDT - 1))
                    return last
                wdeps = [st_wol, st_owl, wo_ev[eb]]
                if sc == 0 and et < 2:
                    wdeps.append(pv["A"])
                st_wom = p.step("tensor", em_wom, wdeps, tag=f"a1_wom{sc}_{et}")
                wo_mm[eb] = st_wom
                last_owin_mm = st_wom

                def em_woe(v, et=et, eb=eb):
                    pb = (psA, psB)[eb]
                    return v.tensor_copy(out=osb[:, eb * 512:(eb + 1) * 512], in_=pb[:, :])
                st_woe = p.step("vector", em_woe, [st_wom, wo_ev[eb]], tag=f"a1_woe{sc}_{et}")

                def em_wos(sy, et=et, eb=eb, sc=sc):
                    return [(sy.dma_start(out=ca_dram[et * 128:(et + 1) * 128,
                                                      sc * 512:(sc + 1) * 512],
                                          in_=osb[:, eb * 512:(eb + 1) * 512]), 16)]
                st_wos = p.dma("sync", em_wos, [st_woe], tag=f"a1_wos{sc}_{et}",
                               sem="a1_ws" + str(eb))
                wo_ev[eb] = st_wos
        return p.last_steps()



def build_residual_phase(cb, xT_dram, ca_dram, cf_dram, out_dram, ca_tok,
                         deps=(), uid="r0"):
    nc, p = cb.nc, cb.p
    with contextlib.ExitStack() as es:
        l = CB(nc, p, es)
        xt = l.sb(uid + "_x", [128, 2 * 512], F32)
        cf = l.sb(uid + "_cf", [128, 2 * 512], F32)
        cat = l.sb(uid + "_ca", [128, 2 * 512], F32)
        ot = l.sb(uid + "_o", [128, 2 * 512], F32)
        id_f = l.sb(uid + "_idf", [128, 128], F32)
        ps = l.ps(uid + "_ps", [128, 1024])

        def em_c(g):
            g.memset(id_f[:, :], 1.0)
            return g.affine_select(out=id_f[:, :], in_=id_f[:, :], pattern=[[-1, 128]],
                                   compare_op=ALU.is_equal, fill=0.0, base=0,
                                   channel_multiplier=1)
        st_c = p.step("gpsimd", em_c, deps, tag=uid + "_c")

        ld_hist = [None, None]
        tr_hist = [None, None]
        add_hist = [None, None]
        st_hist = [None, None]
        last = None
        for it in range(64):
            dt, sc = it // 4, it % 4
            b = it % 2
            bo = b * 512

            def em_ld(g, dt=dt, sc=sc, bo=bo):
                r = [(g.dma_start(out=xt[:, bo:bo + 512],
                                  in_=xT_dram[dt * 128:(dt + 1) * 128,
                                              sc * 512:(sc + 1) * 512]), 16),
                     (g.dma_start(out=cf[:, bo:bo + 512],
                                  in_=cf_dram[dt * 128:(dt + 1) * 128,
                                              sc * 512:(sc + 1) * 512]), 16)]
                if not ca_tok:
                    r.append((g.dma_start(out=cat[:, bo:bo + 512],
                                          in_=ca_dram[dt * 128:(dt + 1) * 128,
                                                      sc * 512:(sc + 1) * 512]), 16))
                else:
                    for i in range(4):
                        r.append((g.dma_start(
                            out=cat[:, bo + i * 128: bo + (i + 1) * 128],
                            in_=ca_dram[sc * 512 + i * 128: sc * 512 + (i + 1) * 128,
                                        dt * 128:(dt + 1) * 128]), 16))
                return r
            n_dma = 3 if not ca_tok else 6
            st_ld = p.dma("gpsimd", em_ld, list(deps) + [add_hist[b]], n=n_dma,
                          tag=f"{uid}_ld{it}", sem=uid + "_ld" + str(b))

            if ca_tok:
                def em_tr(t, bo=bo, b=b):
                    lastt = None
                    for i in range(4):
                        lastt = t.transpose(apx(ps, b * 512 + i * 128, [(1, 128)]),
                                            cat[:, bo + i * 128: bo + (i + 1) * 128],
                                            id_f[:, :])
                    return lastt
                st_tr = p.step("tensor", em_tr, [st_ld, st_c, add_hist[b]],
                               tag=f"{uid}_tr{it}")
                tr_hist[b] = st_tr

                def em_add(v, bo=bo, b=b):
                    v.tensor_tensor(out=ot[:, bo:bo + 512], in0=apx(ps, b * 512, [(1, 512)]),
                                    in1=xt[:, bo:bo + 512], op=ALU.add)
                    return v.tensor_tensor(out=ot[:, bo:bo + 512], in0=ot[:, bo:bo + 512],
                                           in1=cf[:, bo:bo + 512], op=ALU.add)
                st_add = p.step("vector", em_add, [st_tr, st_ld, st_hist[b]],
                                tag=f"{uid}_ad{it}")
            else:
                def em_add1(v, bo=bo):
                    return v.tensor_tensor(out=ot[:, bo:bo + 512], in0=cat[:, bo:bo + 512],
                                           in1=xt[:, bo:bo + 512], op=ALU.add)
                st_add1 = p.step("vector", em_add1, [st_ld, st_hist[b]],
                                 tag=f"{uid}_a1{it}")

                def em_add2(v, bo=bo):
                    return v.tensor_tensor(out=ot[:, bo:bo + 512], in0=ot[:, bo:bo + 512],
                                           in1=cf[:, bo:bo + 512], op=ALU.add)
                st_add = p.step("vector", em_add2, [st_add1, st_ld], tag=f"{uid}_ad{it}")
            add_hist[b] = st_add

            def em_st(sy, dt=dt, sc=sc, bo=bo):
                return [(sy.dma_start(out=out_dram[dt * 128:(dt + 1) * 128,
                                                   sc * 512:(sc + 1) * 512],
                                      in_=ot[:, bo:bo + 512]), 16)]
            st_st = p.dma("sync", em_st, [st_add], tag=f"{uid}_st{it}",
                          sem=uid + "_st" + str(b))
            st_hist[b] = st_st
            last = st_st
        return p.last_steps()


from concourse.bass_utils import run_bass_kernel_spmd

NCORES = 8


def _pm_w1(W):
    t = W.reshape(64, 128, 16, 128)
    return np.ascontiguousarray(t.transpose(3, 0, 2, 1).reshape(128, 64, 2048))


def _pm_w2(W):
    t = W.reshape(16, 128, 64, 128)
    return np.ascontiguousarray(t.transpose(3, 0, 2, 1).reshape(128, 16, 8192))


def _pm_sq(W):
    t = W.reshape(16, 128, 16, 128)
    return np.ascontiguousarray(t.transpose(3, 0, 2, 1).reshape(128, 16, 2048))


def _run_phase(build_fn, io_specs, in_maps):
    nc = bass.Bass("TRN2", target_bir_lowering=False)
    aps = {}
    for name, shape, dtype, kind in io_specs:
        aps[name] = nc.dram_tensor(name, list(shape), dtype, kind=kind).ap()
    prog = Prog(nc)
    with contextlib.ExitStack() as es:
        cb = CB(nc, prog, es)
        build_fn(cb, aps)
        prog.emit_block()
    res = run_bass_kernel_spmd(nc, in_maps, core_ids=list(range(NCORES)))
    return res.results


def kernel(L0_attn_x, L0_attn_qkv_w, L0_attn_conv_w0, L0_attn_conv_b0,
           L0_attn_conv_w1, L0_attn_conv_b1, L0_attn_proj_w, L0_attn_proj_b,
           L0_attn_ln_w, L0_attn_ln_b, L0_ffn_w_gate, L0_ffn_w_up, L0_ffn_w_down,
           L1_attn_wq, L1_attn_wk, L1_attn_wv, L1_attn_wo,
           L1_attn_fq1_w, L1_attn_fq1_b, L1_attn_fq2_w, L1_attn_fq2_b,
           L1_attn_fk1_w, L1_attn_fk1_b, L1_attn_fk2_w, L1_attn_fk2_b,
           L1_attn_norm_w, L1_ffn_proj1, L1_ffn_proj1_bias, L1_ffn_proj2,
           L1_ffn_proj2_bias, L1_ffn_gate_w):
    f32 = np.float32
    asn = np.ascontiguousarray
    xT = [asn(np.asarray(L0_attn_x[c], f32).T) for c in range(NCORES)]

    l0w = {
        "qkv_w": np.asarray(L0_attn_qkv_w, f32),
        "cw0": asn(np.asarray(L0_attn_conv_w0, f32)[:, 0, 1, :]),
        "cb0": np.asarray(L0_attn_conv_b0, f32),
        "cw1": asn(np.asarray(L0_attn_conv_w1, f32)[:, 0, :, 1]),
        "cb1": np.asarray(L0_attn_conv_b1, f32),
        "proj_w": np.asarray(L0_attn_proj_w, f32),
        "proj_b": np.asarray(L0_attn_proj_b, f32),
        "ln_w": np.asarray(L0_attn_ln_w, f32),
        "ln_b": np.asarray(L0_attn_ln_b, f32),
    }
    wg_pm = _pm_w1(np.asarray(L0_ffn_w_gate, f32))
    wu_pm = _pm_w1(np.asarray(L0_ffn_w_up, f32))
    wd_pm = _pm_w2(np.asarray(L0_ffn_w_down, f32))
    l1w = {
        "wq_pm": _pm_sq(np.asarray(L1_attn_wq, f32)),
        "wk_pm": _pm_sq(np.asarray(L1_attn_wk, f32)),
        "wv_pm": _pm_sq(np.asarray(L1_attn_wv, f32)),
        "wo_pm": _pm_sq(np.asarray(L1_attn_wo, f32)),
        "fq1T": asn(np.asarray(L1_attn_fq1_w, f32).T),
        "fq2T": asn(np.asarray(L1_attn_fq2_w, f32).T),
        "fk1T": asn(np.asarray(L1_attn_fk1_w, f32).T),
        "fk2T": asn(np.asarray(L1_attn_fk2_w, f32).T),
        "fq1_b": np.asarray(L1_attn_fq1_b, f32).reshape(128, 1),
        "fq2_b": np.asarray(L1_attn_fq2_b, f32).reshape(128, 1),
        "fk1_b": np.asarray(L1_attn_fk1_b, f32),
        "fk2_b": np.asarray(L1_attn_fk2_b, f32),
        "norm_w": np.asarray(L1_attn_norm_w, f32).reshape(128, 1),
    }
    p1_pm = _pm_w1(np.asarray(L1_ffn_proj1, f32))
    p2_pm = _pm_w2(np.asarray(L1_ffn_proj2, f32))
    b1c = asn(np.asarray(L1_ffn_proj1_bias, f32).reshape(64, 128).T)
    b2c = asn(np.asarray(L1_ffn_proj2_bias, f32).reshape(16, 128).T)

    EI, EO = "ExternalInput", "ExternalOutput"

    r = _run_phase(lambda cb, a: build_ln_phase(cb, a["xT"], a["xn"]),
                   [("xT", (2048, 2048), F32, EI), ("xn", (2048, 2048), BF16, EO)],
                   [{"xT": xT[c]} for c in range(NCORES)])
    xn0 = [r[c]["xn"] for c in range(NCORES)]

    r = _run_phase(lambda cb, a: build_ffn_phase(cb, a["xn"], a["out"], "swiglu",
                                                 a["wg"], a["wu"], a["wd"], uid="f0"),
                   [("xn", (2048, 2048), BF16, EI), ("wg", wg_pm.shape, F32, EI),
                    ("wu", wu_pm.shape, F32, EI), ("wd", wd_pm.shape, F32, EI),
                    ("out", (2048, 2048), F32, EO)],
                   [{"xn": xn0[c], "wg": wg_pm, "wu": wu_pm, "wd": wd_pm}
                    for c in range(NCORES)])
    cf0 = [r[c]["out"] for c in range(NCORES)]

    l0_specs = [("xn", (2048, 2048), BF16, EI), ("ca", (2048, 2048), F32, EO)]
    for nm, arr in l0w.items():
        l0_specs.append((nm, arr.shape, F32, EI))
    r = _run_phase(lambda cb, a: build_l0attn_phase(
                       cb, a["xn"], a["ca"], {k: a[k] for k in l0w}),
                   l0_specs,
                   [dict(xn=xn0[c], **l0w) for c in range(NCORES)])
    ca0 = [r[c]["ca"] for c in range(NCORES)]

    r = _run_phase(lambda cb, a: build_residual_phase(cb, a["xT"], a["ca"], a["cf"],
                                                      a["out"], True, uid="r0"),
                   [("xT", (2048, 2048), F32, EI), ("ca", (2048, 2048), F32, EI),
                    ("cf", (2048, 2048), F32, EI), ("out", (2048, 2048), F32, EO)],
                   [{"xT": xT[c], "ca": ca0[c], "cf": cf0[c]} for c in range(NCORES)])
    x1T = [r[c]["out"] for c in range(NCORES)]
    del ca0, cf0

    r = _run_phase(lambda cb, a: build_ln_phase(cb, a["xT"], a["xn"], uid="1"),
                   [("xT", (2048, 2048), F32, EI), ("xn", (2048, 2048), BF16, EO)],
                   [{"xT": x1T[c]} for c in range(NCORES)])
    xn1 = [r[c]["xn"] for c in range(NCORES)]

    l1_specs = [("xn", (2048, 2048), BF16, EI), ("ca", (2048, 2048), F32, EO)]
    for nm, arr in l1w.items():
        l1_specs.append((nm, arr.shape, F32, EI))
    r = _run_phase(lambda cb, a: build_l1attn_phase(
                       cb, a["xn"], a["ca"], {k: a[k] for k in l1w}),
                   l1_specs,
                   [dict(xn=xn1[c], **l1w) for c in range(NCORES)])
    ca1 = [r[c]["ca"] for c in range(NCORES)]

    r = _run_phase(lambda cb, a: build_ffn_phase(cb, a["xn"], a["out"], "gelu",
                                                 a["w1"], None, a["w2"],
                                                 b1_dram=a["b1"], b2_dram=a["b2"],
                                                 uid="f1"),
                   [("xn", (2048, 2048), BF16, EI), ("w1", p1_pm.shape, F32, EI),
                    ("w2", p2_pm.shape, F32, EI), ("b1", (128, 64), F32, EI),
                    ("b2", (128, 16), F32, EI), ("out", (2048, 2048), F32, EO)],
                   [{"xn": xn1[c], "w1": p1_pm, "w2": p2_pm, "b1": b1c, "b2": b2c}
                    for c in range(NCORES)])
    cf1 = [r[c]["out"] for c in range(NCORES)]
    del xn0, xn1

    r = _run_phase(lambda cb, a: build_residual_phase(cb, a["xT"], a["ca"], a["cf"],
                                                      a["out"], False, uid="r1"),
                   [("xT", (2048, 2048), F32, EI), ("ca", (2048, 2048), F32, EI),
                    ("cf", (2048, 2048), F32, EI), ("out", (2048, 2048), F32, EO)],
                   [{"xT": x1T[c], "ca": ca1[c], "cf": cf1[c]} for c in range(NCORES)])
    out = np.stack([r[c]["out"].T for c in range(NCORES)], axis=0)
    return np.ascontiguousarray(out.astype(np.float32))
